# version 08
# baseline (speedup 1.0000x reference)
import numpy as np
import concourse.bass as bass
import concourse.mybir as mybir
from contextlib import ExitStack

F32 = mybir.dt.float32
BF16 = mybir.dt.bfloat16
AF = mybir.ActivationFunctionType
ALU = mybir.AluOpType
AX = mybir.AxisListType
ET = mybir.EngineType

MAGIC = float(np.float32(1.5 * 2 ** 23))
D = 1024
DI = 2048
QSTEP = float(4.0 / (np.sqrt(2048.0 * 16) * 7.0))
NLNQ = float(-np.log(QSTEP))
EPS = 1e-5


class Sem:
    def __init__(self, nc, es, name):
        self.h = es.enter_context(nc.semaphore(name))
        self.n = 0

    def inc(self, instr, amt=1):
        instr.then_inc(self.h, amt)
        self.n += amt

    def mark(self, eng, amt=1):
        eng.maybe_drain_then_inc((self.h, amt))
        self.n += amt

    def wait(self, eng, n=None):
        eng.wait_ge(self.h, self.n if n is None else n)


def stage_weights(inp):
    w = {}
    w["w_ip"] = np.ascontiguousarray(inp["L0_attn_in_proj_w"].T)
    w["b_ip"] = np.ascontiguousarray(inp["L0_attn_in_proj_b"].reshape(32, 128).T)
    w["normw"] = np.ascontiguousarray(inp["L0_attn_norm_w"].reshape(8, 128).T)
    w["normb"] = np.ascontiguousarray(inp["L0_attn_norm_b"].reshape(8, 128).T)
    cw = inp["L0_attn_conv_w"][:, 0, :]
    w["convw"] = np.ascontiguousarray(cw.reshape(16, 128, 4).transpose(1, 2, 0).reshape(128, 64))
    w["convb"] = np.ascontiguousarray(inp["L0_attn_conv_b"].reshape(16, 128).T)
    w["w_xp"] = np.ascontiguousarray(inp["L0_attn_x_proj_w"].T)
    w["b_xp"] = np.ascontiguousarray(inp["L0_attn_x_proj_b"].reshape(33, 1))
    w["dtw"] = np.ascontiguousarray(inp["L0_attn_dt_proj_w"][:, 0].reshape(16, 128).T)
    w["dtb"] = np.ascontiguousarray(inp["L0_attn_dt_proj_b"].reshape(16, 128).T)
    w["alog"] = np.ascontiguousarray(inp["L0_attn_A_log"].reshape(1, 16))
    w["dvec"] = np.ascontiguousarray(inp["L0_attn_D"].reshape(16, 128).T)
    w["w_op"] = np.ascontiguousarray(inp["L0_attn_out_proj_w"].T)
    w["b_op"] = np.ascontiguousarray(inp["L0_attn_out_proj_b"].reshape(8, 128).T)
    w["gatew"] = np.ascontiguousarray(inp["L0_ffn_gate_w"].T)
    w["w1"] = np.ascontiguousarray(inp["L0_ffn_w1"])
    w["b1"] = np.ascontiguousarray(inp["L0_ffn_b1"].reshape(2, 32, 128).transpose(0, 2, 1))
    w["w2"] = np.ascontiguousarray(inp["L0_ffn_w2"])
    w["b2"] = np.ascontiguousarray(inp["L0_ffn_b2"].reshape(2, 8, 128).transpose(0, 2, 1))
    w["sw1"] = np.ascontiguousarray(inp["L0_ffn_sw1"])
    w["sb1"] = np.ascontiguousarray(inp["L0_ffn_sb1"].reshape(32, 128).T)
    w["sw2"] = np.ascontiguousarray(inp["L0_ffn_sw2"])
    w["sb2"] = np.ascontiguousarray(inp["L0_ffn_sb2"].reshape(8, 128).T)
    w["wq"] = np.ascontiguousarray(np.swapaxes(inp["L1_attn_W_q"], 1, 2))
    w["wk"] = np.ascontiguousarray(np.swapaxes(inp["L1_attn_W_k"], 1, 2))
    w["wv"] = np.ascontiguousarray(np.swapaxes(inp["L1_attn_W_v"], 1, 2))
    w["wo"] = np.ascontiguousarray(inp["L1_attn_W_o"])
    for nm, key in (("ssma", "L1_attn_SSM_A"), ("ssmb", "L1_attn_SSM_B"), ("ssmc", "L1_attn_SSM_C")):
        M = inp[key]
        w[nm] = np.ascontiguousarray(np.stack(
            [M[n * 128:(n + 1) * 128, n * 128:(n + 1) * 128] for n in range(8)]))
    w["lng"] = np.ascontiguousarray(inp["L1_attn_ln_gamma"].reshape(8, 128).T)
    w["lnb"] = np.ascontiguousarray(inp["L1_attn_ln_beta"].reshape(8, 128).T)
    w["l1w"] = np.ascontiguousarray(inp["L1_ffn_weight"].T)
    w["l1wr"] = np.ascontiguousarray(inp["L1_ffn_weight"])
    w["l1b"] = np.ascontiguousarray(inp["L1_ffn_bias"].reshape(8, 128).T)
    w["eyein"] = np.eye(128, dtype=np.float32)
    return w


class Ctx:
    pass


def build_program(S, debug_outs=(), stop_after=None):
    TB = 512
    NSB = S // TB
    TC = 32
    NCH = S // TC
    NQ = S // 4
    QCH = NCH // 4
    NT128 = S // 128

    nc = bass.Bass("TRN2", target_bir_lowering=False, debug=False, detect_race_conditions=False)
    c = Ctx()
    c.nc = nc
    c.S, c.TB, c.NSB, c.TC, c.NCH, c.NQ, c.QCH, c.NT128 = S, TB, NSB, TC, NCH, NQ, QCH, NT128
    c.debug = set(debug_outs)
    c.stop_after = stop_after

    din = lambda n, sh, dt=F32: nc.dram_tensor(n, sh, dt, kind="ExternalInput").ap()
    c.x_full = din("x_full", [S, D])
    c.w_ip = din("w_ip", [D, 2 * DI]); c.b_ip = din("b_ip", [128, 32])
    c.normw = din("normw", [128, 8]); c.normb = din("normb", [128, 8])
    c.convw = din("convw", [128, 64]); c.convb = din("convb", [128, 16])
    c.w_xp = din("w_xp", [DI, 33]); c.b_xp = din("b_xp", [33, 1])
    c.dtw = din("dtw", [128, 16]); c.dtb = din("dtb", [128, 16])
    c.alog = din("alog", [1, 16]); c.dvec = din("dvec", [128, 16])
    c.w_op = din("w_op", [DI, D]); c.b_op = din("b_op", [128, 8])
    c.gatew = din("gatew", [D, 2])
    c.w1 = din("w1", [2, D, 4096]); c.b1 = din("b1", [2, 128, 32])
    c.w2 = din("w2", [2, 4096, D]); c.b2 = din("b2", [2, 128, 8])
    c.sw1 = din("sw1", [D, 4096]); c.sb1 = din("sb1", [128, 32])
    c.sw2 = din("sw2", [4096, D]); c.sb2 = din("sb2", [128, 8])
    c.wq = din("wq", [8, 128, 128]); c.wk = din("wk", [8, 128, 128]); c.wv = din("wv", [8, 128, 128])
    c.wo = din("wo", [D, D])
    c.ssma = din("ssma", [8, 128, 128]); c.ssmb = din("ssmb", [8, 128, 128]); c.ssmc = din("ssmc", [8, 128, 128])
    c.lng = din("lng", [128, 8]); c.lnb = din("lnb", [128, 8])
    c.l1w = din("l1w", [D, D]); c.l1wr = din("l1wr", [D, D]); c.l1b = din("l1b", [128, 8])
    c.eyein = din("eyein", [128, 128])

    c.out_q = nc.dram_tensor("out_q", [NQ, D], F32, kind="ExternalOutput").ap()

    def dbg(name, shape, dt=F32):
        return nc.dram_tensor("dbg_" + name, shape, dt, kind="ExternalOutput").ap() if name in c.debug else None

    c.dbg_z = dbg("z", [16, 128, S], BF16)
    c.dbg_xn = dbg("xn", [8, 128, S])
    c.dbg_xconv = dbg("xconv", [128, S * 16], BF16)
    c.dbg_dtt = dbg("dtt", [128, S * 16], BF16)
    c.dbg_bc = dbg("bc", [33, S])
    c.dbg_y = dbg("y", [128, NQ * 16])
    c.dbg_coefs = dbg("coefs", [128, NT128 * 4])

    c.sc_xconv = nc.dram_tensor("sc_xconv", [128, S * 16], BF16).ap()
    c.sc_dtt = nc.dram_tensor("sc_dtt", [128, S * 16], BF16).ap()
    c.sc_bc = nc.dram_tensor("sc_bc", [33, S], F32).ap()
    c.sc_z = nc.dram_tensor("sc_z", [16, 128, S], BF16).ap()
    c.sc_xn = nc.dram_tensor("sc_xn", [8, 128, S], F32).ap()
    c.sc_y = nc.dram_tensor("sc_y", [128, NQ * 16], F32).ap()
    c.sc_h = nc.dram_tensor("sc_h", [32, 128, NQ], BF16).ap()
    c.sc_xnl = nc.dram_tensor("sc_xnl", [8, 128, NQ], F32).ap()
    c.sc_l0 = nc.dram_tensor("sc_l0", [8, 128, NQ], F32).ap()
    c.cc_in = nc.dram_tensor("cc_in", [1, 64], F32).ap()
    c.cc_out = nc.dram_tensor("cc_out", [1, 64], F32).ap()
    c.sw_ip = nc.dram_tensor("sw_ip", [D, 2 * DI], BF16).ap()
    c.sw_w1 = nc.dram_tensor("sw_w1", [2, D, 4096], BF16).ap()
    c.sw_w2 = nc.dram_tensor("sw_w2", [2, 4096, D], BF16).ap()
    c.sw_sw1 = nc.dram_tensor("sw_sw1", [D, 4096], BF16).ap()
    c.sw_sw2 = nc.dram_tensor("sw_sw2", [4096, D], BF16).ap()
    c.sw_op = nc.dram_tensor("sw_op", [DI, D], BF16).ap()
    c.sw_gw = nc.dram_tensor("sw_gw", [D, 2], BF16).ap()

    es = ExitStack()
    c.es = es
    c.cur_es = es
    c.sb = lambda n, sh, dt=F32: c.cur_es.enter_context(nc.sbuf_tensor(n, sh, dt))
    c.ps = lambda n, sh: c.cur_es.enter_context(nc.psum_tensor(n, sh, F32))
    c.sem = lambda n: Sem(nc, es, n)

    with nc.Block() as _blk:
        emit_all(c)
    es.close()
    return nc, c


def emit_all(c):
    nc = c.nc
    S, TB, NSB, TC, NCH, NQ, QCH, NT128 = c.S, c.TB, c.NSB, c.TC, c.NCH, c.NQ, c.QCH, c.NT128
    sy, v, t_, s_, g_ = nc.sync, nc.vector, nc.tensor, nc.scalar, nc.gpsimd

    sb, ps, sem = c.sb, c.ps, c.sem
    ones = sb("ones", [128, 128])
    eye = sb("eye", [128, 128])
    coefs = sb("coefs", [128, NT128 * 4])
    sCW = sb("sCW", [128, 64]); sCB = sb("sCB", [128, 16])
    sDTW = sb("sDTW", [128, 16]); sDTB = sb("sDTB", [128, 16])
    sNW = sb("sNW", [128, 8]); sNB = sb("sNB", [128, 8])
    sIPB = sb("sIPB", [128, 32])
    sALr = sb("sALr", [1, 16])
    sAneg = sb("sAneg", [128, 16])
    sDV = sb("sDV", [128, 16])
    sBXP = sb("sBXP", [33, 1])
    c_lnq = sb("c_lnq", [128, 1])
    c_eps = sb("c_eps", [128, 1])
    c_tiny = sb("c_tiny", [128, 1])
    c_plnq = sb("c_plnq", [128, 1])

    s_wld = sem("s_wld")
    s_p0 = sem("s_p0")
    c.s_p0sem = s_p0

    s_cast = sem("s_cast")
    s_cast.inc(g_.dma_start(out=c.sw_ip, in_=c.w_ip), 16)
    for e in range(2):
        s_cast.inc(g_.dma_start(out=c.sw_w1[e], in_=c.w1[e]), 16)
        s_cast.inc(g_.dma_start(out=c.sw_w2[e], in_=c.w2[e]), 16)
    s_cast.inc(g_.dma_start(out=c.sw_sw1, in_=c.sw1), 16)
    s_cast.inc(g_.dma_start(out=c.sw_sw2, in_=c.sw2), 16)
    s_cast.inc(g_.dma_start(out=c.sw_op, in_=c.w_op), 16)
    s_cast.inc(g_.dma_start(out=c.sw_gw, in_=c.gatew), 16)
    c.s_cast = s_cast

    wloads = [(eye[:], c.eyein), (sCW[:], c.convw), (sCB[:], c.convb),
              (sDTW[:], c.dtw), (sDTB[:], c.dtb), (sNW[:], c.normw), (sNB[:], c.normb),
              (sIPB[:], c.b_ip), (sALr[:], c.alog), (sDV[:], c.dvec), (sBXP[:], c.b_xp)]
    for dst, src in wloads:
        s_wld.inc(sy.dma_start(out=dst, in_=src), 16)

    v.memset(ones[:], 1.0)
    v.memset(c_lnq[:], NLNQ)
    v.memset(c_eps[:], EPS)
    v.memset(c_tiny[:], 1e-16)
    v.memset(c_plnq[:], -NLNQ)
    s_p0.mark(v)
    p_misc = ps("p_misc", [128, 512])
    s_wld.wait(s_)
    s_.activation(out=sALr[:], in_=sALr[:], func=AF.Exp)
    s_p0.mark(s_)
    s_wld.wait(t_)
    s_p0.wait(t_)
    t_.matmul(out=p_misc[:, 0:16], lhsT=ones[0:1, :], rhs=sALr[:], start=True, stop=True)
    s_p0.mark(t_)
    s_p0.wait(v)
    v.tensor_scalar(out=sAneg[:], in0=p_misc[:, 0:16], scalar1=-1.0, scalar2=None, op0=ALU.mult)
    s_p0.mark(v)

    p1_es = ExitStack()
    c.cur_es = p1_es
    xa = sb("xa", [128, 2 * D])
    tA = sb("tA", [128, 8])
    tjunk = sb("tjunk", [128, D])
    s_lda = sem("s_lda")
    s_va = sem("s_va")
    s_sa = sem("s_sa")
    s_vafree = sem("s_vafree")

    for i in range(NT128 // 2):
        if i >= 1:
            s_vafree.wait(sy)
        s_lda.inc(sy.dma_start(out=xa[:, 0:D], in_=c.x_full[i * 256:i * 256 + 128, :]), 16)
        s_lda.inc(sy.dma_start(out=xa[:, D:2 * D], in_=c.x_full[i * 256 + 128:(i + 1) * 256, :]), 16)
        s_lda.wait(v)
        for j in range(2):
            tt = 2 * i + j
            xt = xa[:, j * D:(j + 1) * D]
            cf = coefs[:, tt * 4:(tt + 1) * 4]
            v.tensor_reduce(out=tA[:, 0:1], in_=xt, axis=AX.X, op=ALU.add)
            v.scalar_tensor_tensor(out=tjunk[:], in0=xt, scalar=1.0, in1=xt,
                                   op0=ALU.mult, op1=ALU.mult, accum_out=tA[:, 1:2])
            v.drain()
            v.tensor_scalar(out=tA[:, 2:3], in0=tA[:, 0:1], scalar1=1.0 / D, scalar2=None, op0=ALU.mult)
            v.drain()
            v.scalar_tensor_tensor(out=tA[:, 4:5], in0=tA[:, 2:3], scalar=1.0, in1=tA[:, 2:3],
                                   op0=ALU.mult, op1=ALU.mult)
            v.tensor_scalar(out=tA[:, 3:4], in0=tA[:, 1:2], scalar1=1.0 / D, scalar2=None, op0=ALU.mult)
            v.drain()
            s_va.inc(v.tensor_tensor(out=tA[:, 3:4], in0=tA[:, 3:4], in1=tA[:, 4:5], op=ALU.subtract))
            s_va.wait(s_)
            s_.activation(out=tA[:, 5:6], in_=tA[:, 3:4], func=AF.Ln, bias=c_eps[:])
            s_.drain()
            s_sa.inc(s_.activation(out=tA[:, 5:6], in_=tA[:, 5:6], func=AF.Exp, scale=-0.5))
            s_sa.wait(v)
            v.tensor_copy(out=cf[:, 0:1], in_=tA[:, 5:6])
            v.scalar_tensor_tensor(out=cf[:, 1:2], in0=tA[:, 2:3], scalar=-1.0, in1=tA[:, 5:6],
                                   op0=ALU.mult, op1=ALU.mult)
            v.scalar_tensor_tensor(out=tA[:, 4:5], in0=tA[:, 5:6], scalar=1.0, in1=tA[:, 5:6],
                                   op0=ALU.mult, op1=ALU.mult)
            v.drain()
            s_va.inc(v.tensor_tensor(out=tA[:, 4:5], in0=tA[:, 3:4], in1=tA[:, 4:5], op=ALU.mult))
            s_va.wait(s_)
            s_.activation(out=tA[:, 6:7], in_=tA[:, 4:5], func=AF.Ln, bias=c_eps[:])
            s_.drain()
            s_sa.inc(s_.activation(out=tA[:, 6:7], in_=tA[:, 6:7], func=AF.Exp, scale=-0.5))
            s_sa.wait(v)
            v.tensor_tensor(out=cf[:, 2:3], in0=tA[:, 5:6], in1=tA[:, 6:7], op=ALU.mult)
            v.drain()
            v.scalar_tensor_tensor(out=cf[:, 3:4], in0=tA[:, 2:3], scalar=-1.0, in1=cf[:, 2:3],
                                   op0=ALU.mult, op1=ALU.mult)
        s_vafree.mark(v)

    xb = sb("xb", [128, 4 * D])
    xg = sb("xg", [128, 4 * D])
    xnt = sb("xnt", [128, 4 * D])
    xn2f = sb("xn2f", [128, 8 * TB], BF16)
    xnf = sb("xnf", [128, 8 * TB])
    Wf = sb("Wf", [128, 8 * 128 * 2], BF16)
    xmw = sb("xmw", [128, 16 * (TB + 3)], BF16)
    zst = sb("zst", [128, 2 * TB], BF16)
    cvt = sb("cvt", [128, 4 * TB])
    xcs = sb("xcs", [128, TB * 16], BF16)
    Wxp = sb("Wxp", [128, 16 * 33], BF16)
    Wxpf = sb("Wxpf", [128, 16 * 33])
    bcst = sb("bcst", [33, TB])
    p_tr = ps("p_tr", [128, 512])
    p_tr2 = ps("p_tr2", [128, 512])
    p_ip = ps("p_ip", [128, 512])
    p_ip2 = ps("p_ip2", [128, 512])
    p_xs = ps("p_xs", [128, 512])

    s_ldb = sem("s_ldb")
    s_ldw = sem("s_ldw")
    s_vb = sem("s_vb")
    s_tb = sem("s_tb")
    s_sb = sem("s_sb")
    s_gb = sem("s_gb")
    c.s_gb_sem = s_gb
    s_evf = sem("s_evf")

    for kc in range(16):
        s_wld.inc(sy.dma_start(out=Wxpf[:, kc * 33:(kc + 1) * 33],
                               in_=c.w_xp[kc * 128:(kc + 1) * 128, :]), 16)
    s_wld.wait(v)
    v.tensor_copy(out=Wxp[:], in_=Wxpf[:])
    s_vb.mark(v)

    v.memset(xmw[:], 0.0)
    s_vb.mark(v)

    for i in range(NSB):
        t0 = i * TB
        if i >= 1:
            s_vb.wait(sy)
        for j in range(4):
            s_ldb.inc(sy.dma_start(out=xb[:, j * D:(j + 1) * D],
                                   in_=c.x_full[t0 + j * 128:t0 + (j + 1) * 128, :]), 16)
        s_ldb.wait(v)
        for j in range(4):
            tt = (t0 // 128) + j
            cf = coefs[:, tt * 4:(tt + 1) * 4]
            xt = xb[:, j * D:(j + 1) * D]
            v.tensor_scalar(out=xg[:, j * D:(j + 1) * D], in0=xt, scalar1=cf[:, 2:3],
                            scalar2=cf[:, 3:4], op0=ALU.mult, op1=ALU.add)
            v.tensor_scalar(out=xnt[:, j * D:(j + 1) * D], in0=xt, scalar1=cf[:, 0:1],
                            scalar2=cf[:, 1:2], op0=ALU.mult, op1=ALU.add)
        s_vb.mark(v)
        s_vb.wait(t_)
        for j in range(4):
            if i > 0 or j > 0:
                s_evf.wait(t_)
            for ch in range(8):
                pb = p_tr if ch < 4 else p_tr2
                t_.transpose(out=pb[:, (ch % 4) * 128:(ch % 4 + 1) * 128],
                             in_=xg[:, j * D + ch * 128:j * D + (ch + 1) * 128], identity=eye[:])
            s_tb.mark(t_)
            s_tb.wait(s_)
            for ch in range(8):
                pb = p_tr if ch < 4 else p_tr2
                s_.activation(out=xn2f[:, ch * TB + j * 128:ch * TB + (j + 1) * 128],
                              in_=pb[:, (ch % 4) * 128:(ch % 4 + 1) * 128],
                              func=AF.Identity, scale=sNW[:, ch:ch + 1], bias=sNB[:, ch:ch + 1])
            s_sb.mark(s_)
            s_sb.wait(t_)
            for ch in range(8):
                pb = p_tr if ch < 4 else p_tr2
                t_.transpose(out=pb[:, (ch % 4) * 128:(ch % 4 + 1) * 128],
                             in_=xnt[:, j * D + ch * 128:j * D + (ch + 1) * 128], identity=eye[:])
            s_tb.mark(t_)
            s_tb.wait(v)
            for ch in range(8):
                pb = p_tr if ch < 4 else p_tr2
                v.tensor_copy(out=xnf[:, ch * TB + j * 128:ch * TB + (j + 1) * 128],
                              in_=pb[:, (ch % 4) * 128:(ch % 4 + 1) * 128])
            s_evf.mark(v)
        s_evf.wait(g_)
        for ch in range(8):
            s_gb.inc(g_.dma_start(out=c.sc_xn[ch, :, t0:t0 + TB], in_=xnf[:, ch * TB:(ch + 1) * TB]), 16)
            if c.dbg_xn is not None:
                s_gb.inc(g_.dma_start(out=c.dbg_xn[ch, :, t0:t0 + TB], in_=xnf[:, ch * TB:(ch + 1) * TB]), 16)
        s_sb.wait(v)
        for g in range(16):
            wg = xmw[:, g * (TB + 3):(g + 1) * (TB + 3)]
            if i > 0:
                v.tensor_copy(out=wg[:, 0:3], in_=wg[:, TB:TB + 3])
        s_vb.mark(v)
        for f in range(32):
            buf = f % 2
            if f >= 2:
                s_tb.wait(sy, s_tb.n)
            if i == 0 and f == 0:
                c.s_cast.wait(sy)
            for kc in range(8):
                s_ldw.inc(sy.dma_start(out=Wf[:, buf * 1024 + kc * 128:buf * 1024 + (kc + 1) * 128],
                                       in_=c.sw_ip[kc * 128:(kc + 1) * 128, f * 128:(f + 1) * 128]), 16)
            s_ldw.wait(t_)
            s_vb.wait(t_)
            pband = p_ip if f % 2 == 0 else p_ip2
            if f >= 2:
                s_evf.wait(t_)
            for kc in range(8):
                t_.matmul(out=pband[:], lhsT=Wf[:, buf * 1024 + kc * 128:buf * 1024 + (kc + 1) * 128],
                          rhs=xn2f[:, kc * TB:(kc + 1) * TB], start=(kc == 0), stop=(kc == 7))
            s_tb.mark(t_)
            s_tb.wait(s_) if f < 16 else s_tb.wait(v)
            if f < 16:
                s_.activation(out=xmw[:, f * (TB + 3) + 3:(f + 1) * (TB + 3)], in_=pband[:],
                              func=AF.Identity, bias=sIPB[:, f:f + 1])
                s_evf.mark(s_)
                s_sb.mark(s_)
            else:
                zslot = zst[:, (f % 2) * TB:(f % 2 + 1) * TB]
                if f >= 18:
                    s_gb.wait(v, s_gb.n - 16 * (2 if c.dbg_z is not None else 1))
                v.tensor_scalar(out=zslot, in0=pband[:], scalar1=sIPB[:, f:f + 1],
                                scalar2=None, op0=ALU.add)
                s_evf.mark(v)
                s_vb.mark(v)
                s_vb.wait(g_)
                s_gb.inc(g_.dma_start(out=c.sc_z[f - 16, :, t0:t0 + TB], in_=zslot), 16)
                if c.dbg_z is not None:
                    s_gb.inc(g_.dma_start(out=c.dbg_z[f - 16, :, t0:t0 + TB], in_=zslot), 16)
        for g in range(16):
            wg = xmw[:, g * (TB + 3):(g + 1) * (TB + 3)]
            s_sb.wait(s_)
            for k in range(4):
                bias = sCB[:, g:g + 1] if k == 3 else None
                if bias is None:
                    s_.activation(out=cvt[:, k * TB:(k + 1) * TB], in_=wg[:, k:k + TB],
                                  func=AF.Identity, scale=sCW[:, k * 16 + g:k * 16 + g + 1])
                else:
                    s_.activation(out=cvt[:, k * TB:(k + 1) * TB], in_=wg[:, k:k + TB],
                                  func=AF.Identity, scale=sCW[:, k * 16 + g:k * 16 + g + 1], bias=bias)
            s_sb.mark(s_)
            s_sb.wait(v)
            v.tensor_tensor(out=cvt[:, 0:TB], in0=cvt[:, 0:TB], in1=cvt[:, TB:2 * TB], op=ALU.add)
            v.tensor_tensor(out=cvt[:, 2 * TB:3 * TB], in0=cvt[:, 2 * TB:3 * TB],
                            in1=cvt[:, 3 * TB:4 * TB], op=ALU.add)
            s_vb.inc(v.tensor_tensor(out=cvt[:, TB:2 * TB], in0=cvt[:, 0:TB],
                                     in1=cvt[:, 2 * TB:3 * TB], op=ALU.add))
            s_vb.wait(s_)
            s_sb.inc(s_.activation(out=xcs[:].rearrange("p (t g) -> p t g", g=16)[:, :, g],
                                   in_=cvt[:, TB:2 * TB], func=AF.Silu))
        s_sb.wait(g_)
        s_gb.inc(g_.dma_start(out=c.sc_xconv[:, t0 * 16:(t0 + TB) * 16], in_=xcs[:]), 16)
        if c.dbg_xconv is not None:
            s_gb.inc(g_.dma_start(out=c.dbg_xconv[:, t0 * 16:(t0 + TB) * 16], in_=xcs[:]), 16)
        s_sb.wait(t_)
        for g in range(16):
            t_.matmul(out=p_xs[0:33, :], lhsT=Wxp[:, g * 33:(g + 1) * 33],
                      rhs=xcs[:].rearrange("p (t g) -> p t g", g=16)[:, :, g],
                      start=(g == 0), stop=(g == 15))
        s_tb.mark(t_)
        s_tb.wait(s_)
        s_.activation(out=bcst[:], in_=p_xs[0:33, :], func=AF.Identity, bias=sBXP[:])
        s_sb.mark(s_)
        s_sb.wait(g_)
        s_gb.inc(g_.dma_start(out=c.sc_bc[:, t0:t0 + TB], in_=bcst[:]), 16)
        if c.dbg_bc is not None:
            s_gb.inc(g_.dma_start(out=c.dbg_bc[:, t0:t0 + TB], in_=bcst[:]), 16)
        s_gb.wait(s_, s_gb.n)

    dtr1 = sb("dtr1", [1, TB])
    ezt = sb("ezt", [128, TB])
    dts = sb("dts", [128, TB * 16], BF16)
    p_dr = ps("p_dr", [128, 512])
    s_ldc = sem("s_ldc")
    s_tc = sem("s_tc")
    s_sc = sem("s_sc")
    s_vc = sem("s_vc")
    s_gc = sem("s_gc")
    c.s_gc_sem = s_gc

    for i in range(NSB):
        t0 = i * TB
        s_gb.wait(sy)
        if i >= 1:
            s_tc.wait(sy)
        s_ldc.inc(sy.dma_start(out=dtr1[:], in_=c.sc_bc[32:33, t0:t0 + TB]), 16)
        s_ldc.wait(t_)
        if i >= 1:
            s_sc.wait(t_)
        t_.matmul(out=p_dr[:], lhsT=ones[0:1, :], rhs=dtr1[:], start=True, stop=True)
        s_tc.mark(t_)
        s_tc.wait(s_)
        for g in range(16):
            s_.activation(out=ezt[:], in_=p_dr[:], func=AF.Exp,
                          scale=sDTW[:, g:g + 1], bias=sDTB[:, g:g + 1])
            s_sc.mark(s_)
            s_sc.wait(v)
            s_vc.inc(v.tensor_scalar(out=ezt[:], in0=ezt[:], scalar1=1.0, scalar2=None, op0=ALU.add))
            s_vc.wait(s_)
            s_sc.inc(s_.activation(out=dts[:].rearrange("p (t g) -> p t g", g=16)[:, :, g],
                                   in_=ezt[:], func=AF.Ln))
        s_sc.wait(g_)
        s_gc.inc(g_.dma_start(out=c.sc_dtt[:, t0 * 16:(t0 + TB) * 16], in_=dts[:]), 16)
        if c.dbg_dtt is not None:
            s_gc.inc(g_.dma_start(out=c.dbg_dtt[:, t0 * 16:(t0 + TB) * 16], in_=dts[:]), 16)
        s_gc.wait(s_, s_gc.n)

    if c.stop_after == "p1":
        finish_dummy(c, locals())
        p1_es.close()
        return

    s_pbar = sem("s_pbar")
    s_gb = c.s_gb_sem; s_gc = c.s_gc_sem
    s_gb.wait(sy); s_gc.wait(sy)
    s_pbar.mark(sy)
    s_pbar.mark(v)
    s_pbar.mark(t_)
    s_pbar.mark(s_)
    s_gb.wait(g_, s_gb.n); s_gc.wait(g_, s_gc.n)
    s_pbar.mark(g_)
    for eng in (sy, v, t_, s_, g_):
        s_pbar.wait(eng)
    c.cur_es = c.es
    p1_es.close()

    emit_scan(c, dict(ones=ones, eye=eye, sAneg=sAneg, c_lnq=c_lnq, sDV=sDV, c_tiny=c_tiny, c_plnq=c_plnq))

    if c.stop_after == "scan":
        finish_dummy(c, locals())
        c.scan_es.close()
        return

    s_pbar2 = sem("s_pbar2")
    qb2 = sy.alloc_register()
    sy.reg_mod(qb2, sy.to_reg(nc.partition_id(engines=[ET.SP])), 4)
    for q in range(4):
        with sy.If_eq(qb2, q):
            c.scan_ydma[q].wait(sy)
    sy.end_ifs()
    s_pbar2.mark(sy); s_pbar2.mark(v); s_pbar2.mark(t_); s_pbar2.mark(s_); s_pbar2.mark(g_)
    for eng in (sy, v, t_, s_, g_):
        s_pbar2.wait(eng)
    c.cur_es = c.es
    c.scan_es.close()

    emit_epilogue(c, dict(ones=ones, eye=eye, sDV=sDV, sAneg=sAneg, c_lnq=c_lnq,
                          c_eps=c_eps, c_tiny=c_tiny, p_misc=p_misc))


def finish_dummy(c, env):
    nc = c.nc
    sy, v, g_ = nc.sync, nc.vector, nc.gpsimd
    s_fin = c.sem("s_fin")
    if c.dbg_coefs is not None:
        sdc = c.sem("sdc")
        sdc.inc(g_.dma_start(out=c.dbg_coefs, in_=env["coefs"][:]), 16)
        sdc.wait(g_)
    zt = c.sb("zf", [128, D])
    v.memset(zt[:], 0.0)
    s_fin.mark(v)
    s_fin.wait(g_)
    s_dout = c.sem("s_dout")
    step = min(128, c.NQ)
    for r in range(0, c.NQ, step):
        s_dout.inc(g_.dma_start(out=c.out_q[r:r + step, :], in_=zt[0:step, :]), 16)
    s_dout.wait(g_)


def emit_scan(c, tiles):
    nc = c.nc
    S, TC, NCH, QCH = c.S, c.TC, c.NCH, c.QCH
    sy, v, t_, s_, g_ = nc.sync, nc.vector, nc.tensor, nc.scalar, nc.gpsimd
    c.scan_es = ExitStack()
    c.cur_es = c.scan_es
    sb, ps, sem = c.sb, c.ps, c.sem
    ones, eye, sAneg, c_lnq = tiles["ones"], tiles["eye"], tiles["sAneg"], tiles["c_lnq"]
    c_tiny = tiles["c_tiny"]
    c_plnq = tiles["c_plnq"]

    HH = sb("HH", [128, 256 * (2 * TC)])
    Hzero = sb("Hzero", [128, 256])
    DA = sb("DA", [128, 256 * TC * 2], BF16)
    UU = sb("UU", [128, 256 * TC * 2], BF16)
    dtc = sb("dtc", [128, TC * 16 * 2], BF16)
    xcc = sb("xcc", [128, TC * 16 * 2], BF16)
    wtc = sb("wtc", [128, TC * 16 * 2], BF16)
    BCt = sb("BCt", [128, 32 * 2], BF16)
    BCr = sb("BCr", [128, TC * 32 * 2], BF16)
    ymb = sb("ymb", [128, 256 * TC], BF16)
    yq = sb("yq", [128, TC * 16])
    m_t = sb("m_t", [128, 256]); p_t = sb("p_t", [128, 256])
    jk = sb("jk", [128, 256]); x1 = sb("x1", [128, 256])
    ssp = sb("ssp", [128, 1]); rt = sb("rt", [128, 1])
    lss = sb("lss", [128, 1]); kt = sb("kt", [128, 1])
    bcf = sb("bcf", [32, 128 * 2])
    p_ss = ps("p_ss", [128, 2])
    p_bc = ps("p_bc", [128, 512])
    p_bc2 = ps("p_bc2", [128, 512])
    p_bt = ps("p_bt", [128, 32])

    s_zd = sem("s_zd")
    s_bcl = sem("s_bcl")
    s_bct = sem("s_bct")
    s_bce = sem("s_bce")
    s_brp = sem("s_brp")
    s_bre = sem("s_bre")
    s_wt = sem("s_wt")
    s_da = sem("s_da")
    s_uu = sem("s_uu")
    s_v2t = sem("s_v2t")
    s_t2s = sem("s_t2s")
    s_s2v = sem("s_s2v")
    s_hst = sem("s_hst")
    s_ymd = sem("s_ymd")
    s_yrd = sem("s_yrd")
    s_ydma = [sem(f"s_ydma{i}") for i in range(4)]
    c.scan_ydma = s_ydma

    v.memset(Hzero[:], 0.0)
    qv = v.alloc_register(); qg = g_.alloc_register(); qsy = sy.alloc_register()
    v.reg_mod(qv, v.to_reg(nc.partition_id(engines=[ET.DVE])), 4)
    g_.reg_mod(qg, g_.to_reg(nc.partition_id(engines=[ET.Pool])), 4)
    sy.reg_mod(qsy, sy.to_reg(nc.partition_id(engines=[ET.SP])), 4)
    s_.wait_ge(c.s_p0sem.h, 4)

    def dma_chunk(j):
        if j >= NCH:
            return
        buf = j % 2
        t0 = j * TC
        if j >= 2:
            s_da.wait(sy, j - 1)
            s_wt.wait(sy, j - 1)
        s_zd.inc(sy.dma_start(out=dtc[:, buf * TC * 16:(buf + 1) * TC * 16],
                              in_=c.sc_dtt[:, t0 * 16:(t0 + TC) * 16]), 16)
        s_zd.inc(sy.dma_start(out=xcc[:, buf * TC * 16:(buf + 1) * TC * 16],
                              in_=c.sc_xconv[:, t0 * 16:(t0 + TC) * 16]), 16)
        if j % 4 == 0:
            gidx = j // 4
            gb = gidx % 2
            if gidx >= 2:
                s_brp.wait(sy, 2 * (4 * (gidx - 2) + 4))
            s_bcl.inc(sy.dma_start(out=bcf[:, gb * 128:(gb + 1) * 128],
                                   in_=c.sc_bc[0:32, t0:t0 + 128]), 16)

    def bct_group(j):
        if j >= NCH or j % 4 != 0:
            return
        gidx = j // 4
        gb = gidx % 2
        s_bcl.wait(t_)
        if gidx >= 1:
            s_bce.wait(t_, gidx)
        t_.transpose(out=p_bt[:], in_=bcf[:, gb * 128:(gb + 1) * 128], identity=eye[0:32, 0:32])
        s_bct.mark(t_)
        s_bct.wait(s_)
        s_.copy(out=BCt[:, gb * 32:(gb + 1) * 32], in_=p_bt[:])
        s_bce.mark(s_)

    def wtilde(j):
        if j >= NCH:
            return
        buf = j % 2
        s_zd.wait(g_)
        g_.tensor_tensor(out=wtc[:, buf * TC * 16:(buf + 1) * TC * 16],
                         in0=dtc[:, buf * TC * 16:(buf + 1) * TC * 16],
                         in1=xcc[:, buf * TC * 16:(buf + 1) * TC * 16], op=ALU.mult)
        s_wt.mark(g_)

    def bc_mm(j, s):
        if j >= NCH:
            return
        buf = j % 2
        gb2 = (j // 4) % 2
        bct = BCt[:, gb2 * 32:(gb2 + 1) * 32]
        if s == 0:
            s_bce.wait(t_)
            if j >= 1:
                s_bre.wait(t_, 2 * j - 1)
        if s == 16 and j >= 1:
            s_bre.wait(t_, 2 * j)
        pb = p_bc if s < 16 else p_bc2
        s16 = s % 16
        ecol = (j * TC + s) % 128
        t_.matmul(out=pb[:, s16 * 32:(s16 + 1) * 32],
                  lhsT=eye[:, ecol:ecol + 1].broadcast_to([128, 128]).bitcast(F32) if False else eyebf[:, ecol:ecol + 1].broadcast_to([128, 128]),
                  rhs=bct, start=(s16 == 0), stop=(s16 == 15))
        if s16 == 15:
            s_brp.mark(t_)

    def bc_evac(j, half):
        if j >= NCH:
            return
        buf = j % 2
        s_brp.wait(s_)
        if j >= 2 and half == 0:
            s_ymd.wait(s_, j - 1)
        pb = p_bc if half == 0 else p_bc2
        s_.activation(out=BCr[:, buf * TC * 32 + half * 512:buf * TC * 32 + (half + 1) * 512],
                      in_=pb[:], func=AF.Identity)
        s_bre.mark(s_)

    def da_act(j, n):
        if j >= NCH:
            return
        buf = j % 2
        if n == 0:
            s_zd.wait(s_)
        s_.activation(
            out=DA[:, buf * TC * 256 + n * TC * 16:buf * TC * 256 + (n + 1) * TC * 16],
            in_=dtc[:, buf * TC * 16:(buf + 1) * TC * 16],
            func=AF.Exp, scale=sAneg[:, n:n + 1])
        if n == 15:
            s_da.mark(s_)

    def u_half(j, half):
        if j >= NCH:
            return
        buf = j % 2
        s_wt.wait(g_)
        s_bre.wait(g_)
        w4 = wtc[:, buf * TC * 16 + half * (TC // 2) * 16:buf * TC * 16 + (half + 1) * (TC // 2) * 16]\
            .rearrange("p (t g) -> p t g", g=16).unsqueeze(3).broadcast_to([128, TC // 2, 16, 16])
        b4 = BCr[:, buf * TC * 32 + half * (TC // 2) * 32:buf * TC * 32 + (half + 1) * (TC // 2) * 32]\
            .rearrange("p (t x) -> p t x", x=32)[:, :, 0:16]\
            .unsqueeze(2).broadcast_to([128, TC // 2, 16, 16])
        g_.tensor_tensor(
            out=UU[:, buf * TC * 256 + half * (TC // 2) * 256:buf * TC * 256 + (half + 1) * (TC // 2) * 256]
                .rearrange("p (t g n) -> p t g n", g=16, n=16),
            in0=w4, in1=b4, op=ALU.mult)
        s_uu.mark(g_)

    eyebf = sb("eyebf", [128, 128], BF16)
    v.tensor_copy(out=eyebf[:], in_=eye[:])
    v.drain()
    s_hst.mark(v)
    s_hst.wait(t_, 1)

    dma_chunk(0)
    dma_chunk(1)
    bct_group(0)
    wtilde(0)
    for s in range(TC):
        bc_mm(0, s)
    bc_evac(0, 0)
    bc_evac(0, 1)
    for n in range(16):
        da_act(0, n)
    u_half(0, 0)
    u_half(0, 1)

    for k in range(NCH):
        buf = k % 2
        t0 = k * TC
        kq = k // QCH
        dma_chunk(k + 2)
        bct_group(k + 1)
        wtilde(k + 1)
        s_da.wait(v, k + 1)
        s_uu.wait(v, 2 * k + 1)
        for s in range(TC):
            t = t0 + s
            slot = t % (2 * TC)
            prev = Hzero[:] if t == 0 else HH[:, ((t - 1) % (2 * TC)) * 256:(((t - 1) % (2 * TC)) + 1) * 256]
            cur = HH[:, slot * 256:(slot + 1) * 256]
            dabase = DA[:, buf * TC * 256:(buf + 1) * TC * 256].rearrange("p (n t g) -> p n t g", n=16, t=TC)
            da = dabase[:, :, s, :].rearrange("p n g -> p g n")
            uu = UU[:, (buf * TC + s) * 256:(buf * TC + s + 1) * 256]
            if s == 0 and k >= 2:
                s_yrd.wait(v, k - 1)
            if s == 16:
                s_uu.wait(v, 2 * k + 2)
            v.tensor_tensor(out=m_t[:].rearrange("p (g n) -> p g n", n=16),
                            in0=prev.rearrange("p (g n) -> p g n", n=16), in1=da, op=ALU.mult)
            v.tensor_tensor(out=p_t[:], in0=m_t[:], in1=uu, op=ALU.add)
            s_v2t.inc(v.scalar_tensor_tensor(out=jk[:], in0=p_t[:], scalar=1.0, in1=p_t[:],
                                             op0=ALU.mult, op1=ALU.mult, accum_out=ssp[:]))
            t_.wait_ge(s_v2t.h, t + 1)
            s_t2s.inc(t_.matmul(out=p_ss[:, 0:1], lhsT=ones[:], rhs=ssp[:], start=True, stop=True))
            bc_mm(k + 1, s)
            s_.wait_ge(s_t2s.h, t + 1)
            s_.activation(out=lss[:], in_=p_ss[:, 0:1], func=AF.Ln, bias=c_tiny[:])
            s_.activation(out=rt[:], in_=lss[:], func=AF.Exp, scale=-0.5, bias=c_lnq[:])
            s_s2v.inc(s_.activation(out=kt[:], in_=lss[:], func=AF.Exp, scale=0.5, bias=c_plnq[:]))
            if s % 2 == 0:
                da_act(k + 1, s // 2)
            if s == 17:
                bc_evac(k + 1, 0)
            if s == 18:
                u_half(k + 1, 0)
            v.wait_ge(s_s2v.h, t + 1)
            v.tensor_scalar(out=x1[:], in0=p_t[:], scalar1=rt[:], scalar2=MAGIC, op0=ALU.mult, op1=ALU.add)
            v.tensor_scalar(out=x1[:], in0=x1[:], scalar1=MAGIC, scalar2=7.0, op0=ALU.subtract, op1=ALU.min)
            v.tensor_scalar(out=cur, in0=x1[:], scalar1=-7.0, scalar2=kt[:], op0=ALU.max, op1=ALU.mult)
        bc_evac(k + 1, 1)
        u_half(k + 1, 1)
        s_hst.mark(v)
        slot0 = (t0 % (2 * TC)) * 256
        s_hst.wait(g_, k + 2)
        with g_.If_eq(qg, kq):
            hh4 = HH[:, slot0:slot0 + TC * 256].rearrange("p (t g n) -> p t g n", g=16, n=16)
            cr4 = BCr[:, buf * TC * 32:(buf + 1) * TC * 32].rearrange("p (t x) -> p t x", x=32)[:, :, 16:32]\
                .unsqueeze(2).broadcast_to([128, TC, 16, 16])
            g_.tensor_tensor(out=ymb[:].rearrange("p (t g n) -> p t g n", g=16, n=16),
                             in0=hh4, in1=cr4, op=ALU.mult)
        g_.end_ifs()
        s_ymd.mark(g_)
        s_ymd.wait(v, k + 1)
        with v.If_eq(qv, kq):
            v.tensor_reduce(out=yq[:], in_=ymb[:].rearrange("p (a n) -> p a n", n=16),
                            axis=AX.X, op=ALU.add)
        v.end_ifs()
        s_yrd.mark(v)
        sy.wait_ge(s_yrd.h, k + 1)
        with sy.If_eq(qsy, kq):
            sq = s_ydma[kq]
            sq.inc(sy.dma_start(out=c.sc_y[:, (k % QCH) * TC * 16:((k % QCH) + 1) * TC * 16],
                                in_=yq[:]), 16)
            if c.dbg_y is not None:
                sq.inc(sy.dma_start(out=c.dbg_y[:, (k % QCH) * TC * 16:((k % QCH) + 1) * TC * 16],
                                    in_=yq[:]), 16)
            sy.wait_ge(sq.h, sq.n)
        sy.end_ifs()
def emit_epilogue(c, tiles):
    nc = c.nc
    S, TB, NQ, QCH = c.S, c.TB, c.NQ, c.QCH
    NQB = NQ // TB
    sy, v, t_, s_, g_ = nc.sync, nc.vector, nc.tensor, nc.scalar, nc.gpsimd
    ep_es = ExitStack()
    c.cur_es = ep_es
    sb, ps, sem = c.sb, c.ps, c.sem
    ones, eye, sDV = tiles["ones"], tiles["eye"], tiles["sDV"]
    c_eps = tiles["c_eps"]

    qsy = sy.alloc_register(); qv2 = v.alloc_register(); qg2 = g_.alloc_register()
    sy.reg_mod(qsy, sy.to_reg(nc.partition_id(engines=[ET.SP])), 4)
    v.reg_mod(qv2, v.to_reg(nc.partition_id(engines=[ET.DVE])), 4)
    g_.reg_mod(qg2, g_.to_reg(nc.partition_id(engines=[ET.Pool])), 4)

    Wst = sb("Wst", [128, 16 * 128 * 2], BF16)
    Wsf = sb("Wsf", [128, 8 * 128 * 2])
    s0s = sb("s0s", [128, TB])
    htmp = sb("htmp", [128, TB])
    sqt = sb("sqt", [128, TB])
    xl0 = sb("xl0", [128, 8 * TB])
    xnl = sb("xnl", [128, 8 * TB])
    qwsb = sb("qwsb", [128, 8 * 1024], BF16)
    wqs = sb("wqs", [128, 3 * 8 * 128])
    ssms = sb("ssms", [128, 3 * 8 * 128])
    lngs = sb("lngs", [128, 8]); lnbs = sb("lnbs", [128, 8])
    l1bs = sb("l1bs", [128, 8]); bops = sb("bops", [128, 8])
    sb1s = sb("sb1s", [128, 32]); sb2s = sb("sb2s", [128, 8])
    b1s = sb("b1s", [128, 32 * 2]); b2s = sb("b2s", [128, 8 * 2])
    cvec = sb("cvec", [128, 8 * 2]); dvecC = sb("dvecC", [128, 8 * 2])
    rb1 = sb("rb1", [128, 32 * 2], BF16)
    stat = sb("stat", [128, 16])
    prow = sb("prow", [1, NQB])
    mxc = sb("mxc", [128, NQB * 8])
    ccs = sb("ccs", [1, 64]); ccb = sb("ccb", [128, 64])
    otm = sb("otm", [128, D])
    pA = ps("pA", [128, 512]); pB = ps("pB", [128, 512])
    pC = ps("pC", [128, 512]); pD = ps("pD", [128, 512])
    pE = ps("pE", [128, 512]); pF = ps("pF", [128, 512])
    pG = ps("pG", [128, 512]); pH = tiles["p_misc"]

    ep1_es = ExitStack()
    c.cur_es = ep1_es
    yb = sb("yb", [128, TB * 16], BF16)
    xcb = sb("xcb", [128, TB * 16], BF16)
    zb1 = sb("zb1", [128, 2 * TB], BF16)
    xnb = sb("xnb", [128, 8 * TB], BF16)
    xlb = sb("xlb", [128, D])
    gat = sb("gat", [128, 16 * TB], BF16)
    sgt = sb("sgt", [128, TB])
    msk = sb("msk", [128, TB])
    s1r = sb("s1r", [1, TB])
    hbt = sb("hbt", [128, 2 * TB], BF16)
    c.cur_es = ep_es

    s_eld = sem("s_eld")
    s_wl2 = sem("s_wl2")
    s_ev = sem("s_ev")
    s_et = sem("s_et")
    s_es = sem("s_es")
    s_ed = sem("s_ed")
    s_cc = sem("s_cc")
    s_out = sem("s_out")
    s_egld = sem("s_egld")

    def bdma(out_fn, in_fn, semm):
        for q in range(4):
            with sy.If_eq(qsy, q):
                semm.inc(sy.dma_start(out=out_fn(q), in_=in_fn(q)), 16)
        sy.end_ifs()
        semm.n -= 16 * 3

    for h in range(8):
        s_wl2.inc(sy.dma_start(out=wqs[:, h * 128:(h + 1) * 128], in_=c.wq[h]), 16)
        s_wl2.inc(sy.dma_start(out=wqs[:, 1024 + h * 128:1024 + (h + 1) * 128], in_=c.wk[h]), 16)
        s_wl2.inc(sy.dma_start(out=wqs[:, 2048 + h * 128:2048 + (h + 1) * 128], in_=c.wv[h]), 16)
        s_wl2.inc(sy.dma_start(out=ssms[:, h * 128:(h + 1) * 128], in_=c.ssma[h]), 16)
        s_wl2.inc(sy.dma_start(out=ssms[:, 1024 + h * 128:1024 + (h + 1) * 128], in_=c.ssmb[h]), 16)
        s_wl2.inc(sy.dma_start(out=ssms[:, 2048 + h * 128:2048 + (h + 1) * 128], in_=c.ssmc[h]), 16)
    for dst, srcw in ((lngs, c.lng), (lnbs, c.lnb), (l1bs, c.l1b), (bops, c.b_op),
                      (sb1s, c.sb1), (sb2s, c.sb2)):
        s_wl2.inc(sy.dma_start(out=dst[:], in_=srcw), 16)
    for e in range(2):
        s_wl2.inc(sy.dma_start(out=b1s[:, e * 32:(e + 1) * 32], in_=c.b1[e]), 16)
        s_wl2.inc(sy.dma_start(out=b2s[:, e * 8:(e + 1) * 8], in_=c.b2[e]), 16)

    v.memset(stat[:, 10:11], 0.0)
    s_ev.mark(v)
    for kc in range(8):
        wb = kc % 2
        if kc >= 2:
            s_es.wait(sy)
        for hh in range(8):
            s_wl2.inc(sy.dma_start(out=Wsf[:, wb * 1024 + hh * 128:wb * 1024 + (hh + 1) * 128],
                                   in_=c.l1w[kc * 128:(kc + 1) * 128, hh * 128:(hh + 1) * 128]), 16)
        s_wl2.wait(s_)
        s_ev.wait(s_)
        s_.activation(out=sqt[:, 0:512], in_=Wsf[:, wb * 1024:wb * 1024 + 512], func=AF.Abs,
                      accum_out=stat[:, 8:9])
        s_.activation(out=sqt[:, 0:512], in_=Wsf[:, wb * 1024 + 512:wb * 1024 + 1024], func=AF.Abs,
                      accum_out=stat[:, 9:10])
        s_es.mark(s_)
        s_es.wait(v)
        v.tensor_tensor(out=stat[:, 11:12], in0=stat[:, 8:9], in1=stat[:, 9:10], op=ALU.add)
        v.drain()
        v.tensor_tensor(out=stat[:, 10:11], in0=stat[:, 10:11], in1=stat[:, 11:12], op=ALU.add)
        s_ev.mark(v)
    s_ev.wait(t_)
    t_.matmul(out=pA[:, 0:1], lhsT=ones[:], rhs=stat[:, 10:11], start=True, stop=True)
    s_et.mark(t_)
    s_et.wait(v)
    v.tensor_scalar(out=stat[:, 0:1], in0=pA[:, 0:1], scalar1=1.0 / (1024.0 * 1024.0), scalar2=None, op0=ALU.mult)
    v.drain()
    v.tensor_scalar(out=stat[:, 1:2], in0=stat[:, 0:1], scalar1=0.5, scalar2=None, op0=ALU.mult)
    v.drain()
    s_ev.mark(v)
    for kc in range(8):
        wb = kc % 2
        if kc >= 2:
            s_ev.wait(sy)
        for hh in range(8):
            s_wl2.inc(sy.dma_start(out=Wsf[:, wb * 1024 + hh * 128:wb * 1024 + (hh + 1) * 128],
                                   in_=c.l1w[kc * 128:(kc + 1) * 128, hh * 128:(hh + 1) * 128]), 16)
        s_wl2.wait(s_)
        s_ev.wait(s_)
        for half in range(2):
            sl = slice(wb * 1024 + half * 512, wb * 1024 + (half + 1) * 512)
            dst = slice(kc * 1024 + half * 512, kc * 1024 + (half + 1) * 512)
            s_.activation(out=sqt[:, 0:512], in_=Wsf[:, sl], func=AF.Abs)
            s_.activation(out=htmp[:, 0:512], in_=Wsf[:, sl], func=AF.Sign)
            s_es.mark(s_)
            s_es.wait(v)
            v.tensor_scalar(out=sqt[:, 0:512], in0=sqt[:, 0:512], scalar1=stat[:, 1:2], scalar2=None, op0=ALU.is_gt)
            v.tensor_tensor(out=qwsb[:, dst], in0=sqt[:, 0:512], in1=htmp[:, 0:512], op=ALU.mult)
            s_ev.mark(v)
            s_ev.wait(s_)
    s_wl2.wait(v)
    for e in range(2):
        v.tensor_relu(out=rb1[:, e * 32:(e + 1) * 32], in_=b1s[:, e * 32:(e + 1) * 32])
    v.memset(prow[:], 0.0)
    v.memset(mxc[:], 0.0)
    s_ev.mark(v)

    s_ev.wait(t_)
    s_ev.wait(sy)
    c.s_cast.wait(sy)
    for e in range(2):
        for f in range(8):
            for hc in range(32):
                wb2 = hc % 2
                if hc >= 2:
                    s_et.wait(sy, s_et.n)
                s_wl2.inc(sy.dma_start(out=Wst[:, wb2 * 2048:wb2 * 2048 + 128],
                                       in_=(c.sw_w2[e])[hc * 128:(hc + 1) * 128, f * 128:(f + 1) * 128]), 16)
                s_wl2.wait(t_)
                t_.matmul(out=pH[:, 0:1], lhsT=Wst[:, wb2 * 2048:wb2 * 2048 + 128],
                          rhs=rb1[:, e * 32 + hc:e * 32 + hc + 1], start=(hc == 0), stop=(hc == 31))
                s_et.mark(t_)
            s_et.wait(v)
            v.tensor_copy(out=cvec[:, e * 8 + f:e * 8 + f + 1], in_=pH[:, 0:1])
            s_ev.mark(v)
            s_ev.wait(t_)
    s_ev.wait(v, s_ev.n)
    v.drain()
    for f in range(8):
        v.tensor_tensor(out=dvecC[:, f:f + 1], in0=b2s[:, f:f + 1], in1=cvec[:, 8 + f:8 + f + 1], op=ALU.add)
        v.tensor_tensor(out=dvecC[:, 8 + f:8 + f + 1], in0=b2s[:, 8 + f:8 + f + 1], in1=cvec[:, f:f + 1], op=ALU.add)
    s_ev.mark(v)

    for i in range(NQB):
        t0q = i * TB
        if i >= 1:
            s_ev.wait(sy)
        else:
            s_ev.wait(sy)
        s_egld.inc(g_.dma_start(out=yb[:], in_=c.sc_y[:, (i * TB) * 16:(i * TB + TB) * 16]), 16)
        bdma(lambda q: xcb[:], lambda q, i=i: c.sc_xconv[:, (q * NQ + i * TB) * 16:(q * NQ + i * TB + TB) * 16], s_eld)
        for ch in range(8):
            for q in range(4):
                with g_.If_eq(qg2, q):
                    s_egld.inc(g_.dma_start(out=xnb[:, ch * TB:(ch + 1) * TB],
                                            in_=c.sc_xn[ch, :, q * NQ + i * TB:q * NQ + i * TB + TB]), 16)
            g_.end_ifs()
            s_egld.n -= 48
        s_eld.wait(v)
        s_egld.wait(v)
        for g in range(16):
            v.scalar_tensor_tensor(out=gat[:, g * TB:(g + 1) * TB],
                                   in0=xcb[:].rearrange("p (t g) -> p t g", g=16)[:, :, g],
                                   scalar=sDV[:, g:g + 1],
                                   in1=yb[:].rearrange("p (t g) -> p t g", g=16)[:, :, g],
                                   op0=ALU.mult, op1=ALU.add)
        s_ev.mark(v)
        for g in range(16):
            zbuf = zb1[:, (g % 2) * TB:(g % 2 + 1) * TB]
            if g >= 2:
                s_es.wait(sy)
            bdma(lambda q, zbuf=zbuf: zbuf,
                 lambda q, g=g, i=i: c.sc_z[g, :, q * NQ + i * TB:q * NQ + i * TB + TB], s_eld)
            s_eld.wait(s_)
            s_.activation(out=sgt[:], in_=zbuf, func=AF.Silu)
            s_es.mark(s_)
            s_es.wait(v)
            v.tensor_tensor(out=gat[:, g * TB:(g + 1) * TB], in0=gat[:, g * TB:(g + 1) * TB],
                            in1=sgt[:], op=ALU.mult)
            s_ev.mark(v)
            s_ev.wait(s_)
        for f in range(8):
            wb = f % 2
            if f >= 2:
                s_et.wait(sy)
            for kc in range(16):
                s_wl2.inc(sy.dma_start(out=Wst[:, wb * 2048 + kc * 128:wb * 2048 + (kc + 1) * 128],
                                       in_=c.sw_op[kc * 128:(kc + 1) * 128, f * 128:(f + 1) * 128]), 16)
            s_wl2.wait(t_)
            s_ev.wait(t_)
            pband = pA if f % 2 == 0 else pB
            for kc in range(16):
                t_.matmul(out=pband[:], lhsT=Wst[:, wb * 2048 + kc * 128:wb * 2048 + (kc + 1) * 128],
                          rhs=gat[:, kc * TB:(kc + 1) * TB], start=(kc == 0), stop=(kc == 15))
            s_et.mark(t_)
            s_et.wait(v)
            v.tensor_scalar(out=htmp[:], in0=pband[:], scalar1=bops[:, f:f + 1], scalar2=None, op0=ALU.add)
            v.tensor_tensor(out=xl0[:, f * TB:(f + 1) * TB], in0=htmp[:], in1=xnb[:, f * TB:(f + 1) * TB], op=ALU.add)
            s_ev.mark(v)
        for j in range(4):
            if True:
                s_ev.wait(sy)
            bdma(lambda q: xlb[:],
                 lambda q, j=j, i=i: c.x_full[q * NQ + i * TB + j * 128:q * NQ + i * TB + (j + 1) * 128, :], s_eld)
            s_eld.wait(t_)
            if j >= 1:
                s_ev.wait(t_, s_ev.n)
            for ch in range(8):
                pb2 = pC if ch < 4 else pD
                t_.transpose(out=pb2[:, (ch % 4) * 128:(ch % 4 + 1) * 128],
                             in_=xlb[:, ch * 128:(ch + 1) * 128], identity=eye[:])
            s_et.mark(t_)
            s_et.wait(v)
            for ch in range(8):
                pb2 = pC if ch < 4 else pD
                v.tensor_tensor(out=xl0[:, ch * TB + j * 128:ch * TB + (j + 1) * 128],
                                in0=xl0[:, ch * TB + j * 128:ch * TB + (j + 1) * 128],
                                in1=pb2[:, (ch % 4) * 128:(ch % 4 + 1) * 128], op=ALU.add)
            s_ev.mark(v)
        for kc in range(8):
            s_wl2.inc(sy.dma_start(out=Wst[:, kc * 2:(kc + 1) * 2], in_=c.sw_gw[kc * 128:(kc + 1) * 128, :]), 16)
        s_wl2.wait(t_)
        s_ev.wait(t_)
        for kc in range(8):
            t_.matmul(out=pE[0:2, 0:TB], lhsT=Wst[:, kc * 2:(kc + 1) * 2],
                      rhs=xnb[:, kc * TB:(kc + 1) * TB], start=(kc == 0), stop=(kc == 7))
        s_et.mark(t_)
        s_et.wait(v)
        v.tensor_copy(out=s0s[0:2, :], in_=pE[0:2, 0:TB])
        s_ev.mark(v)
        s_ev.wait(sy)
        s_eld.inc(sy.dma_start(out=s1r[0:1, :], in_=s0s[1:2, :]), 16)
        s_eld.wait(t_)
        t_.matmul(out=pF[:, 0:TB], lhsT=ones[0:1, :], rhs=s0s[0:1, :], start=True, stop=True)
        t_.matmul(out=pG[:, 0:TB], lhsT=ones[0:1, :], rhs=s1r[0:1, :], start=True, stop=True)
        s_et.mark(t_)
        s_et.wait(v)
        v.tensor_copy(out=s0s[:], in_=pF[:, 0:TB])
        v.tensor_tensor(out=msk[:], in0=pG[:, 0:TB], in1=s0s[:], op=ALU.is_gt)
        s_ev.mark(v)
        for e in range(3):
            w1src = c.sw_sw1 if e == 2 else c.sw_w1[e]
            b1col = sb1s if e == 2 else b1s[:, e * 32:(e + 1) * 32]
            for hf in range(32):
                wb = hf % 2
                if hf >= 2 or e > 0:
                    s_et.wait(sy)
                for kc in range(8):
                    s_wl2.inc(sy.dma_start(out=Wst[:, wb * 2048 + kc * 128:wb * 2048 + (kc + 1) * 128],
                                           in_=w1src[kc * 128:(kc + 1) * 128, hf * 128:(hf + 1) * 128]), 16)
                s_wl2.wait(t_)
                pband = pA if hf % 2 == 0 else pB
                if hf >= 2 or e > 0:
                    s_es.wait(t_, s_es.n)
                for kc in range(8):
                    t_.matmul(out=pband[:], lhsT=Wst[:, wb * 2048 + kc * 128:wb * 2048 + (kc + 1) * 128],
                              rhs=xnb[:, kc * TB:(kc + 1) * TB], start=(kc == 0), stop=(kc == 7))
                s_et.mark(t_)
                s_et.wait(s_)
                if hf >= 2 or e > 0:
                    s_ed.wait(s_, s_ed.n - 16)
                s_.activation(out=hbt[:, (hf % 2) * TB:(hf % 2 + 1) * TB], in_=pband[:], func=AF.Relu,
                              bias=b1col[:, hf:hf + 1])
                s_es.mark(s_)
                s_es.wait(g_)
                s_ed.inc(g_.dma_start(out=c.sc_h[hf, :, t0q:t0q + TB],
                                      in_=hbt[:, (hf % 2) * TB:(hf % 2 + 1) * TB]), 16)
                if e < 2:
                    pass
            for f in range(8):
                wb = f % 2
                s_et.wait(sy)
                for hc in range(16):
                    s_wl2.inc(sy.dma_start(out=Wst[:, wb * 2048 + hc * 128:wb * 2048 + (hc + 1) * 128],
                                           in_=(c.sw_sw2 if e == 2 else c.sw_w2[e])[hc * 128:(hc + 1) * 128, f * 128:(f + 1) * 128]), 16)
                pband = pA if f % 2 == 0 else pB
                s_ed.wait(sy, s_ed.n)
                hload_marks = {}
                for half in range(2):
                    if half == 1:
                        s_et.wait(sy, hload_marks[15])
                        for hc2 in range(16):
                            kcg2 = 16 + hc2
                            s_wl2.inc(sy.dma_start(out=Wst[:, wb * 2048 + hc2 * 128:wb * 2048 + (hc2 + 1) * 128],
                                                   in_=(c.sw_sw2 if e == 2 else c.sw_w2[e])[kcg2 * 128:(kcg2 + 1) * 128, f * 128:(f + 1) * 128]), 16)
                    s_wl2.wait(t_)
                    for hc in range(16):
                        kcg = half * 16 + hc
                        hb = kcg % 2
                        if kcg >= 2:
                            s_et.wait(sy, hload_marks[kcg - 2])
                        s_eld.inc(sy.dma_start(out=hbt[:, hb * TB:(hb + 1) * TB],
                                               in_=c.sc_h[kcg, :, t0q:t0q + TB]), 16)
                        s_eld.wait(t_)
                        t_.matmul(out=pband[:], lhsT=Wst[:, wb * 2048 + hc * 128:wb * 2048 + (hc + 1) * 128],
                                  rhs=hbt[:, hb * TB:(hb + 1) * TB], start=(kcg == 0), stop=(kcg == 31))
                        s_et.mark(t_)
                        hload_marks[kcg] = s_et.n
                s_et.wait(v)
                if e == 0:
                    v.tensor_scalar(out=htmp[:], in0=pband[:], scalar1=dvecC[:, f:f + 1], scalar2=None, op0=ALU.add)
                    v.tensor_tensor(out=sgt[:], in0=htmp[:], in1=msk[:], op=ALU.mult)
                    v.tensor_tensor(out=gat[:, f * TB:(f + 1) * TB], in0=htmp[:], in1=sgt[:], op=ALU.subtract)
                elif e == 1:
                    v.tensor_scalar(out=htmp[:], in0=pband[:], scalar1=dvecC[:, 8 + f:8 + f + 1], scalar2=None, op0=ALU.add)
                    v.tensor_tensor(out=sgt[:], in0=htmp[:], in1=msk[:], op=ALU.mult)
                    v.tensor_tensor(out=gat[:, f * TB:(f + 1) * TB], in0=gat[:, f * TB:(f + 1) * TB],
                                    in1=sgt[:], op=ALU.add)
                else:
                    v.tensor_scalar(out=htmp[:], in0=pband[:], scalar1=sb2s[:, f:f + 1], scalar2=None, op0=ALU.add)
                    v.tensor_tensor(out=gat[:, f * TB:(f + 1) * TB], in0=gat[:, f * TB:(f + 1) * TB],
                                    in1=htmp[:], op=ALU.add)
                s_ev.mark(v)
        s_ev.wait(v, s_ev.n)
        for f in range(8):
            v.tensor_tensor(out=xl0[:, f * TB:(f + 1) * TB], in0=xl0[:, f * TB:(f + 1) * TB],
                            in1=gat[:, f * TB:(f + 1) * TB], op=ALU.add)
        s_ev.mark(v)
        s_ev.wait(g_)
        for f in range(8):
            s_ed.inc(g_.dma_start(out=c.sc_l0[f, :, t0q:t0q + TB], in_=xl0[:, f * TB:(f + 1) * TB]), 16)
        s_ev.wait(t_)
        for f in range(8):
            t_.matmul(out=pC[:, 0:TB], lhsT=ones[:], rhs=xl0[:, f * TB:(f + 1) * TB],
                      start=(f == 0), stop=(f == 7))
        s_et.mark(t_)
        s_et.wait(s_)
        for f in range(8):
            s_.activation(out=sqt[:], in_=xl0[:, f * TB:(f + 1) * TB], func=AF.Square)
            s_es.mark(s_)
            s_es.wait(t_)
            t_.matmul(out=pD[:, 0:TB], lhsT=ones[:], rhs=sqt[:], start=(f == 0), stop=(f == 7))
            s_et.mark(t_)
            s_et.wait(s_, s_et.n)
        s_et.wait(v)
        v.tensor_scalar(out=s0s[:], in0=pC[:, 0:TB], scalar1=1.0 / 1024.0, scalar2=None, op0=ALU.mult)
        v.drain()
        v.scalar_tensor_tensor(out=htmp[:], in0=s0s[:], scalar=1.0, in1=s0s[:], op0=ALU.mult, op1=ALU.mult)
        v.tensor_scalar(out=sqt[:], in0=pD[:, 0:TB], scalar1=1.0 / 1024.0, scalar2=None, op0=ALU.mult)
        v.drain()
        v.tensor_tensor(out=sqt[:], in0=sqt[:], in1=htmp[:], op=ALU.subtract)
        s_ev.mark(v)
        s_ev.wait(s_)
        s_.activation(out=sqt[:], in_=sqt[:], func=AF.Ln, bias=c_eps[:])
        s_.drain()
        s_.activation(out=sqt[:], in_=sqt[:], func=AF.Exp, scale=-0.5)
        s_es.mark(s_)
        s_es.wait(v)
        for f in range(8):
            v.tensor_tensor(out=xnl[:, f * TB:(f + 1) * TB], in0=xl0[:, f * TB:(f + 1) * TB],
                            in1=s0s[:], op=ALU.subtract)
            v.tensor_tensor(out=xnl[:, f * TB:(f + 1) * TB], in0=xnl[:, f * TB:(f + 1) * TB],
                            in1=sqt[:], op=ALU.mult)
        s_ev.mark(v)
        s_ev.wait(g_)
        for f in range(8):
            s_ed.inc(g_.dma_start(out=c.sc_xnl[f, :, t0q:t0q + TB], in_=xnl[:, f * TB:(f + 1) * TB]), 16)
        s_ev.wait(t_)
        for h in range(8):
            t_.matmul(out=pE[:, 0:TB], lhsT=wqs[:, h * 128:(h + 1) * 128],
                      rhs=xnl[:, h * TB:(h + 1) * TB], start=True, stop=True)
            t_.matmul(out=pF[:, 0:TB], lhsT=wqs[:, 1024 + h * 128:1024 + (h + 1) * 128],
                      rhs=xnl[:, h * TB:(h + 1) * TB], start=True, stop=True)
            s_et.mark(t_)
            s_et.wait(v)
            v.tensor_copy(out=htmp[:], in_=pE[:, 0:TB])
            v.tensor_tensor(out=htmp[:], in0=htmp[:], in1=pF[:, 0:TB], op=ALU.mult)
            if h == 0:
                v.tensor_copy(out=s0s[:], in_=htmp[:])
            else:
                v.tensor_tensor(out=s0s[:], in0=s0s[:], in1=htmp[:], op=ALU.add)
            s_ev.mark(v)
            s_ev.wait(t_)
        t_.matmul(out=pG[0:1, 0:TB], lhsT=ones[:, 0:1], rhs=s0s[:], start=True, stop=True)
        s_et.mark(t_)
        s_et.wait(s_)
        s_.activation(out=sqt[0:1, 0:TB], in_=pG[0:1, 0:TB], func=AF.Identity, accum_out=prow[0:1, i:i + 1])
        s_es.mark(s_)
        s_ev.wait(v, s_ev.n)
        s_es.wait(s_, s_es.n)
        for f in range(8):
            s_.activation(out=htmp[:], in_=xnl[:, f * TB:(f + 1) * TB], func=AF.Abs)
            s_es.mark(s_)
            s_es.wait(v)
            v.tensor_scalar(out=sqt[:], in0=htmp[:], scalar1=1.0, scalar2=None,
                            op0=ALU.mult, op1=ALU.max, accum_out=mxc[:, i * 8 + f:i * 8 + f + 1])
            s_ev.mark(v)
            s_ev.wait(s_)
        s_ev.mark(v)
    s_ev.wait(v, s_ev.n)
    v.tensor_reduce(out=stat[:, 2:3], in_=mxc[:], axis=AX.X, op=ALU.max)
    s_ev.mark(v)
    s_ev.wait(t_)
    t_.transpose(out=pH[0:1, 0:128], in_=stat[:, 2:3], identity=eye[:])
    s_et.mark(t_)
    s_et.wait(v)
    s_es.wait(v)
    v.tensor_reduce(out=s0s[0:1, 0:1], in_=pH[0:1, 0:128], axis=AX.X, op=ALU.max)
    v.tensor_reduce(out=s0s[0:1, 1:2], in_=prow[0:1, 0:NQB], axis=AX.X, op=ALU.add)
    v.memset(ccs[:], 0.0)
    v.drain()
    s_ev.mark(v)
    rk_v = v.alloc_register()
    v.reg_mov(rk_v, v.to_reg(nc.partition_id(engines=[ET.DVE])))
    for rk in range(8):
        with v.If_eq(rk_v, rk):
            v.tensor_copy(out=ccs[0:1, 2 * rk:2 * rk + 1], in_=s0s[0:1, 1:2])
            v.tensor_copy(out=ccs[0:1, 2 * rk + 1:2 * rk + 2], in_=s0s[0:1, 0:1])
    v.end_ifs()
    s_ev.mark(v)
    s_ev.wait(sy)
    s_eld.inc(sy.dma_start(out=c.cc_in[:], in_=ccs[:]), 16)
    s_eld.wait(g_)
    s_cc.inc(g_.collective_compute("AllReduce", ALU.add, replica_groups=[list(range(8))],
                                   ins=[c.cc_in.opt()], outs=[c.cc_out.opt()]))
    s_cc.wait(g_)
    s_eld2 = sem("s_eld2")
    s_eld2.inc(g_.dma_start(out=ccs[:], in_=c.cc_out[:]), 16)
    s_eld2.wait(t_)
    t_.matmul(out=pH[:, 0:64], lhsT=ones[0:1, :], rhs=ccs[:], start=True, stop=True)
    s_et.mark(t_)
    s_et.wait(v)
    v.tensor_copy(out=ccb[:], in_=pH[:, 0:64])
    v.drain()
    v.tensor_reduce(out=stat[:, 3:4], in_=ccb[:, 0:16].rearrange("p (a b) -> p a b", b=2)[:, :, 1],
                    axis=AX.X, op=ALU.max)
    v.tensor_reduce(out=stat[:, 4:5], in_=ccb[:, 0:8].rearrange("p (a b) -> p a b", b=2)[:, :, 0],
                    axis=AX.X, op=ALU.add)
    v.tensor_reduce(out=stat[:, 5:6], in_=ccb[:, 8:16].rearrange("p (a b) -> p a b", b=2)[:, :, 0],
                    axis=AX.X, op=ALU.add)
    v.drain()
    s_ev.mark(v)
    SLOG = float(np.log(4.0 * NQ + 1.0))
    s_ev.wait(s_)
    s_.activation(out=stat[:, 6:7], in_=stat[:, 4:5], func=AF.Exp, scale=-10.0 / (4.0 * NQ) / SLOG)
    s_.activation(out=stat[:, 7:8], in_=stat[:, 5:6], func=AF.Exp, scale=-10.0 / (4.0 * NQ) / SLOG)
    s_es.mark(s_)
    s_es.wait(v)
    v.tensor_scalar(out=stat[:, 6:7], in0=stat[:, 6:7], scalar1=1.0, scalar2=None, op0=ALU.add)
    v.tensor_scalar(out=stat[:, 7:8], in0=stat[:, 7:8], scalar1=1.0, scalar2=None, op0=ALU.add)
    v.drain()
    v.reciprocal(out=stat[:, 6:7], in_=stat[:, 6:7])
    v.reciprocal(out=stat[:, 7:8], in_=stat[:, 7:8])
    v.drain()
    v.tensor_tensor(out=stat[:, 8:9], in0=stat[:, 6:7], in1=stat[:, 7:8], op=ALU.add)
    v.drain()
    v.tensor_scalar(out=stat[:, 8:9], in0=stat[:, 8:9], scalar1=0.5, scalar2=None, op0=ALU.mult)
    v.drain()
    v.tensor_scalar(out=stat[:, 9:10], in0=stat[:, 8:9], scalar1=0.99, scalar2=None, op0=ALU.is_gt)
    v.tensor_scalar(out=stat[:, 10:11], in0=stat[:, 8:9], scalar1=0.01, scalar2=None, op0=ALU.is_lt)
    v.drain()
    rk_v2 = v.alloc_register()
    v.reg_mov(rk_v2, v.to_reg(nc.partition_id(engines=[ET.DVE])))
    bt_v = v.alloc_register()
    v.reg_div(bt_v, rk_v2, 4)
    with v.If_eq(bt_v, 0):
        v.tensor_copy(out=stat[:, 11:12], in_=stat[:, 6:7])
    v.end_ifs()
    with v.If_eq(bt_v, 1):
        v.tensor_copy(out=stat[:, 11:12], in_=stat[:, 7:8])
    v.end_ifs()
    v.tensor_scalar(out=stat[:, 12:13], in0=stat[:, 9:10], scalar1=-1.0, scalar2=1.0, op0=ALU.mult, op1=ALU.add)
    v.tensor_scalar(out=stat[:, 13:14], in0=stat[:, 10:11], scalar1=-1.0, scalar2=1.0, op0=ALU.mult, op1=ALU.add)
    v.drain()
    v.tensor_tensor(out=stat[:, 12:13], in0=stat[:, 12:13], in1=stat[:, 13:14], op=ALU.mult)
    v.drain()
    v.tensor_tensor(out=stat[:, 12:13], in0=stat[:, 12:13], in1=stat[:, 11:12], op=ALU.mult)
    v.drain()
    v.tensor_tensor(out=stat[:, 12:13], in0=stat[:, 12:13], in1=stat[:, 9:10], op=ALU.add)
    v.drain()
    v.tensor_scalar(out=stat[:, 13:14], in0=stat[:, 12:13], scalar1=0.1, scalar2=None, op0=ALU.mult)
    v.tensor_scalar(out=stat[:, 14:15], in0=stat[:, 12:13], scalar1=-1.0, scalar2=1.0, op0=ALU.mult, op1=ALU.add)
    v.reciprocal(out=stat[:, 15:16], in_=stat[:, 3:4])
    v.drain()
    v.tensor_scalar(out=stat[:, 15:16], in0=stat[:, 15:16], scalar1=127.0, scalar2=None, op0=ALU.mult)
    v.tensor_scalar(out=stat[:, 2:3], in0=stat[:, 3:4], scalar1=1.0 / 127.0, scalar2=None, op0=ALU.mult)
    v.drain()
    v.tensor_tensor(out=stat[:, 2:3], in0=stat[:, 2:3], in1=stat[:, 0:1], op=ALU.mult)
    v.drain()
    s_ev.mark(v)

    s_pb3 = sem("s_pb3")
    s_pb3.mark(sy); s_pb3.mark(v); s_pb3.mark(t_); s_pb3.mark(s_); s_pb3.mark(g_)
    for eng in (sy, v, t_, s_, g_):
        s_pb3.wait(eng)
    ep1_es.close()
    c.cur_es = ep_es
    eqs = sb("eqs", [128, 8 * TB])
    qinb = sb("qinb", [128, 8 * TB], BF16)
    qkv3 = sb("qkv3", [128, 3 * TB])
    gat2 = sb("gat2", [128, 8 * TB])
    for i in range(NQB):
        t0q = i * TB
        s_ev.wait(sy)
        for f in range(8):
            s_eld.inc(sy.dma_start(out=xl0[:, f * TB:(f + 1) * TB], in_=c.sc_l0[f, :, t0q:t0q + TB]), 16)
            s_eld.inc(sy.dma_start(out=xnl[:, f * TB:(f + 1) * TB], in_=c.sc_xnl[f, :, t0q:t0q + TB]), 16)
        s_eld.wait(t_)
        for n in range(8):
            t_.matmul(out=pA[:, 0:TB], lhsT=wqs[:, n * 128:(n + 1) * 128],
                      rhs=xnl[:, n * TB:(n + 1) * TB], start=True, stop=True)
            t_.matmul(out=pB[:, 0:TB], lhsT=wqs[:, 1024 + n * 128:1024 + (n + 1) * 128],
                      rhs=xnl[:, n * TB:(n + 1) * TB], start=True, stop=True)
            t_.matmul(out=pC[:, 0:TB], lhsT=wqs[:, 2048 + n * 128:2048 + (n + 1) * 128],
                      rhs=xnl[:, n * TB:(n + 1) * TB], start=True, stop=True)
            s_et.mark(t_)
            s_et.wait(v)
            v.tensor_copy(out=qkv3[:, 0:TB], in_=pA[:, 0:TB])
            v.tensor_copy(out=qkv3[:, TB:2 * TB], in_=pB[:, 0:TB])
            v.tensor_copy(out=qkv3[:, 2 * TB:3 * TB], in_=pC[:, 0:TB])
            s_ev.mark(v)
            s_ev.wait(t_)
            t_.matmul(out=pA[:, 0:TB], lhsT=ssms[:, n * 128:(n + 1) * 128],
                      rhs=qkv3[:, 0:TB], start=True, stop=True)
            t_.matmul(out=pB[:, 0:TB], lhsT=ssms[:, 1024 + n * 128:1024 + (n + 1) * 128],
                      rhs=qkv3[:, TB:2 * TB], start=True, stop=True)
            t_.matmul(out=pC[:, 0:TB], lhsT=ssms[:, 2048 + n * 128:2048 + (n + 1) * 128],
                      rhs=qkv3[:, 2 * TB:3 * TB], start=True, stop=True)
            t_.matmul(out=pD[0:1, 0:TB], lhsT=ones[:, 0:1], rhs=qkv3[:, 0:TB], start=True, stop=True)
            t_.matmul(out=pD[32:33, 0:TB], lhsT=ones[:, 0:1], rhs=qkv3[:, TB:2 * TB], start=True, stop=True, tile_position=(0, 32))
            t_.matmul(out=pD[64:65, 0:TB], lhsT=ones[:, 0:1], rhs=qkv3[:, 2 * TB:3 * TB], start=True, stop=True, tile_position=(0, 64))
            s_et.mark(t_)
            s_et.wait(v)
            v.tensor_copy(out=htmp[0:1, :], in_=pD[0:1, 0:TB])
            v.tensor_copy(out=htmp[32:33, :], in_=pD[32:33, 0:TB])
            v.tensor_copy(out=htmp[64:65, :], in_=pD[64:65, 0:TB])
            s_ev.mark(v)
            s_ev.wait(t_)
            t_.matmul(out=pE[:, 0:TB], lhsT=ones[0:1, :], rhs=htmp[0:1, :], start=True, stop=True)
            t_.matmul(out=pF[:, 0:TB], lhsT=ones[32:33, :], rhs=htmp[32:33, :], start=True, stop=True)
            t_.matmul(out=pG[:, 0:TB], lhsT=ones[64:65, :], rhs=htmp[64:65, :], start=True, stop=True)
            s_et.mark(t_)
            s_et.wait(v)
            v.tensor_copy(out=qkv3[:, 0:TB], in_=pA[:, 0:TB])
            v.tensor_copy(out=qkv3[:, TB:2 * TB], in_=pB[:, 0:TB])
            v.tensor_copy(out=qkv3[:, 2 * TB:3 * TB], in_=pC[:, 0:TB])
            v.tensor_tensor(out=htmp[:], in0=qkv3[:, 0:TB], in1=pE[:, 0:TB], op=ALU.mult)
            v.tensor_tensor(out=s0s[:], in0=qkv3[:, TB:2 * TB], in1=pF[:, 0:TB], op=ALU.mult)
            v.tensor_tensor(out=htmp[:], in0=htmp[:], in1=s0s[:], op=ALU.add)
            v.tensor_tensor(out=s0s[:], in0=qkv3[:, 2 * TB:3 * TB], in1=pG[:, 0:TB], op=ALU.mult)
            v.tensor_tensor(out=htmp[:], in0=htmp[:], in1=s0s[:], op=ALU.add)
            v.tensor_tensor(out=sqt[:], in0=qkv3[:, 0:TB], in1=qkv3[:, TB:2 * TB], op=ALU.add)
            v.tensor_tensor(out=sqt[:], in0=sqt[:], in1=qkv3[:, 2 * TB:3 * TB], op=ALU.add)
            v.tensor_scalar(out=htmp[:], in0=htmp[:], scalar1=stat[:, 13:14], scalar2=None, op0=ALU.mult)
            v.tensor_scalar(out=sqt[:], in0=sqt[:], scalar1=stat[:, 14:15], scalar2=None, op0=ALU.mult)
            v.tensor_tensor(out=htmp[:], in0=htmp[:], in1=sqt[:], op=ALU.add)
            v.tensor_tensor(out=eqs[:, n * TB:(n + 1) * TB], in0=htmp[:], in1=xnl[:, n * TB:(n + 1) * TB], op=ALU.add)
            s_ev.mark(v)
            s_ev.wait(t_)
        for f in range(8):
            t_.matmul(out=pC[:, 0:TB], lhsT=ones[:], rhs=eqs[:, f * TB:(f + 1) * TB],
                      start=(f == 0), stop=(f == 7))
        s_et.mark(t_)
        s_et.wait(s_)
        for f in range(8):
            s_.activation(out=sqt[:], in_=eqs[:, f * TB:(f + 1) * TB], func=AF.Square)
            s_es.mark(s_)
            s_es.wait(t_)
            t_.matmul(out=pD[:, 0:TB], lhsT=ones[:], rhs=sqt[:], start=(f == 0), stop=(f == 7))
            s_et.mark(t_)
            s_et.wait(s_, s_et.n)
        s_et.wait(v)
        v.tensor_scalar(out=s0s[:], in0=pC[:, 0:TB], scalar1=1.0 / 1024.0, scalar2=None, op0=ALU.mult)
        v.drain()
        v.scalar_tensor_tensor(out=htmp[:], in0=s0s[:], scalar=1.0, in1=s0s[:], op0=ALU.mult, op1=ALU.mult)
        v.tensor_scalar(out=sqt[:], in0=pD[:, 0:TB], scalar1=1.0 / 1024.0, scalar2=None, op0=ALU.mult)
        v.drain()
        v.tensor_tensor(out=sqt[:], in0=sqt[:], in1=htmp[:], op=ALU.subtract)
        s_ev.mark(v)
        s_ev.wait(s_)
        s_.activation(out=sqt[:], in_=sqt[:], func=AF.Ln, bias=c_eps[:])
        s_.drain()
        s_.activation(out=sqt[:], in_=sqt[:], func=AF.Exp, scale=-0.5)
        s_es.mark(s_)
        s_es.wait(v)
        for f in range(8):
            v.tensor_tensor(out=htmp[:], in0=eqs[:, f * TB:(f + 1) * TB], in1=s0s[:], op=ALU.subtract)
            v.tensor_tensor(out=htmp[:], in0=htmp[:], in1=sqt[:], op=ALU.mult)
            v.tensor_scalar(out=eqs[:, f * TB:(f + 1) * TB], in0=htmp[:], scalar1=lngs[:, f:f + 1],
                            scalar2=lnbs[:, f:f + 1], op0=ALU.mult, op1=ALU.add)
        for f in range(8):
            v.tensor_scalar(out=htmp[:], in0=xnl[:, f * TB:(f + 1) * TB], scalar1=stat[:, 15:16],
                            scalar2=MAGIC, op0=ALU.mult, op1=ALU.add)
            v.tensor_scalar(out=qinb[:, f * TB:(f + 1) * TB], in0=htmp[:], scalar1=MAGIC, scalar2=None,
                            op0=ALU.subtract)
        s_ev.mark(v)
        s_ev.wait(t_)
        for f in range(8):
            wb = f % 2
            if f >= 2:
                s_et.wait(sy)
            for kc in range(8):
                s_wl2.inc(sy.dma_start(out=Wsf[:, wb * 1024 + kc * 128:wb * 1024 + (kc + 1) * 128],
                                       in_=c.wo[kc * 128:(kc + 1) * 128, f * 128:(f + 1) * 128]), 16)
            s_wl2.wait(t_)
            pat = pA if f % 2 == 0 else pB
            pft = pC if f % 2 == 0 else pD
            if f >= 2:
                s_ev.wait(t_, s_ev.n)
            for kc in range(8):
                t_.matmul(out=pat[:, 0:TB], lhsT=Wsf[:, wb * 1024 + kc * 128:wb * 1024 + (kc + 1) * 128],
                          rhs=eqs[:, kc * TB:(kc + 1) * TB], start=(kc == 0), stop=(kc == 7))
            for kc in range(8):
                t_.matmul(out=pft[:, 0:TB], lhsT=qwsb[:, kc * 1024 + f * 128:kc * 1024 + (f + 1) * 128],
                          rhs=qinb[:, kc * TB:(kc + 1) * TB], start=(kc == 0), stop=(kc == 7))
            s_et.mark(t_)
            s_et.wait(v)
            v.tensor_scalar(out=htmp[:], in0=pft[:, 0:TB], scalar1=stat[:, 2:3], scalar2=l1bs[:, f:f + 1],
                            op0=ALU.mult, op1=ALU.add)
            v.tensor_tensor(out=htmp[:], in0=htmp[:], in1=pat[:, 0:TB], op=ALU.add)
            v.tensor_tensor(out=gat2[:, f * TB:(f + 1) * TB], in0=htmp[:], in1=xl0[:, f * TB:(f + 1) * TB], op=ALU.add)
            s_ev.mark(v)
        s_ev.wait(t_)
        for j in range(4):
            for ch in range(8):
                pb2 = pE if ch < 4 else pF
                t_.transpose(out=pb2[:, (ch % 4) * 128:(ch % 4 + 1) * 128],
                             in_=gat2[:, ch * TB + j * 128:ch * TB + (j + 1) * 128], identity=eye[:])
            s_et.mark(t_)
            s_et.wait(v)
            s_out.wait(v)
            for ch in range(8):
                pb2 = pE if ch < 4 else pF
                v.tensor_copy(out=otm[:, ch * 128:(ch + 1) * 128], in_=pb2[:, (ch % 4) * 128:(ch % 4 + 1) * 128])
            s_ev.mark(v)
            s_ev.wait(sy)
            s_out.inc(sy.dma_start(out=c.out_q[t0q + j * 128:t0q + (j + 1) * 128, :], in_=otm[:]), 16)
            s_ev.wait(t_)
    s_out.wait(sy)
    s_out.wait(g_)
    ep_es.close()


_CACHE = {}


def kernel(**inputs):
    from concourse.bass_utils import run_bass_kernel_spmd
    S = inputs["L0_attn_x"].shape[1]
    w = stage_weights(inputs)
    if S not in _CACHE:
        _CACHE[S] = build_program(S)
    nc, c = _CACHE[S]
    x = np.ascontiguousarray(inputs["L0_attn_x"])
    in_maps = []
    for core in range(8):
        b = core // 4
        im = dict(w)
        im["x_full"] = x[b]
        in_maps.append(im)
    res = run_bass_kernel_spmd(nc, in_maps, core_ids=list(range(8)))
    NQ = S // 4
    out = np.empty((2, S, 1024), np.float32)
    for core in range(8):
        b, q = core // 4, core % 4
        out[b, q * NQ:(q + 1) * NQ] = res.results[core]["out_q"]
    return out
